# version 01
# baseline (speedup 1.0000x reference)
import contextlib
import numpy as np
import concourse.bass as bass
import concourse.mybir as mybir
from concourse.bass_utils import run_bass_kernel_spmd

F32 = mybir.dt.float32
BF16 = mybir.dt.bfloat16
I32 = mybir.dt.int32
U8 = mybir.dt.uint8
AF = mybir.ActivationFunctionType
ALU = mybir.AluOpType
AX = mybir.AxisListType

NCORES = 8
SET_LN = 6
SET_SIG = 2
SET_GELU = 11
LNP = 16


def load_act_set(nc, eng, idx):
    eng.add_instruction(mybir.InstLoadActFuncSet(
        name=nc.get_next_instruction_name(), act_func_set_id=idx, ins=[], outs=[]))


class Cfg:
    def __init__(self, T, C, N):
        assert T % 128 == 0 and C % 256 == 0 and N % 1024 == 0
        self.T, self.C, self.N = T, C, N
        self.D = 2 * C
        self.TT = T // 128
        self.CT = C // 128
        self.NT = N // 128
        self.TCW = min(T, 512)
        self.TC = T // self.TCW
        self.JU = C // 2
        self.BU = C // 32
        self.Ds = self.D // NCORES
        self.Ns = self.N // NCORES
        self.Cs = self.C // NCORES
        self.NG7 = N // 512
        self.HH = C // 2
        self.HCW = min(self.HH, 512)
        self.HC = self.HH // self.HCW
        self.G8 = min(N, 1024)
        self.NG8 = N // self.G8
        self.NT8 = self.G8 // 128


def build_program(cfg, stop_after=None):
    T, C, D, N = cfg.T, cfg.C, cfg.D, cfg.N
    TT, CT, NT = cfg.TT, cfg.CT, cfg.NT
    TC, TCW = cfg.TC, cfg.TCW
    HH, HC, HCW = cfg.HH, cfg.HC, cfg.HCW
    JU, BU = cfg.JU, cfg.BU
    nc = bass.Bass("TRN2", target_bir_lowering=False, detect_race_conditions=False)

    x_in = nc.dram_tensor("x", [T, C], F32, kind="ExternalInput")
    W_sl = nc.dram_tensor("W_sl", [cfg.Ds, C], F32, kind="ExternalInput")
    b_in = nc.dram_tensor("b", [1, D], F32, kind="ExternalInput")
    gW_sl = nc.dram_tensor("gW_sl", [cfg.Ns, C // 2], I32, kind="ExternalInput")
    gS_sl = nc.dram_tensor("gS_sl", [cfg.Ns, C // 32], F32, kind="ExternalInput")
    uW_sl = nc.dram_tensor("uW_sl", [cfg.Ns, C // 2], I32, kind="ExternalInput")
    uS_sl = nc.dram_tensor("uS_sl", [cfg.Ns, C // 32], F32, kind="ExternalInput")
    dW_sl = nc.dram_tensor("dW_sl", [cfg.Cs, N // 2], I32, kind="ExternalInput")
    dS_sl = nc.dram_tensor("dS_sl", [cfg.Cs, N // 32], F32, kind="ExternalInput")
    out = nc.dram_tensor("out", [T, C], F32, kind="ExternalOutput")

    Wbf_in = nc.dram_tensor("Wbf_in", [cfg.Ds, C], BF16)
    Wbf = nc.dram_tensor("Wbf", [D, C], BF16)
    gbf_in = nc.dram_tensor("gbf_in", [cfg.Ns, C], BF16)
    gbf = nc.dram_tensor("gbf", [N, C], BF16)
    ubf_in = nc.dram_tensor("ubf_in", [cfg.Ns, C], BF16)
    ubf = nc.dram_tensor("ubf", [N, C], BF16)
    dbf_in = nc.dram_tensor("dbf_in", [cfg.Cs, N], BF16)
    dbf = nc.dram_tensor("dbf", [C, N], BF16)
    ln1d = nc.dram_tensor("ln1d", [T, C], BF16)
    ln2d = nc.dram_tensor("ln2d", [T, C], BF16)
    hhi = nc.dram_tensor("hhi", [C, T], BF16)
    hlo = nc.dram_tensor("hlo", [C, T], BF16)
    x2d = nc.dram_tensor("x2d", [T, C], F32)
    Id = nc.dram_tensor("Id", [N, T], BF16)
    dhalf = nc.dram_tensor("dhalf", [T, HH], F32)

    top = contextlib.ExitStack()
    sem = lambda name: top.enter_context(nc.semaphore(name))
    sbT = lambda st, name, shape, dt: st.enter_context(nc.sbuf_tensor(name, shape, dt))
    psT = lambda st, name, shape: st.enter_context(nc.psum_tensor(name, shape, F32))

    with top:
        c_m1 = sbT(top, "c_m1", [128, 1], F32)
        c_75 = sbT(top, "c_75", [128, 1], F32)
        c_eps = sbT(top, "c_eps", [128, 1], F32)
        c_mh = sbT(top, "c_mh", [128, 1], F32)
        c_zero = sbT(top, "c_zero", [128, 1], F32)
        ones_bf = sbT(top, "ones_bf", [1, 512], BF16)
        b_bf = sbT(top, "b_bf", [1, D], BF16)
        NCOL = 4 * TT
        st_sum = sbT(top, "st_sum", [128, NCOL], F32)
        st_sq = sbT(top, "st_sq", [128, NCOL], F32)
        st_m = sbT(top, "st_m", [128, NCOL], F32)
        st_v0 = sbT(top, "st_v0", [128, NCOL], F32)
        st_nv = sbT(top, "st_nv", [128, NCOL], F32)
        st_ln = sbT(top, "st_ln", [128, NCOL], F32)
        st_r = sbT(top, "st_r", [128, NCOL], F32)
        st_nmr = sbT(top, "st_nmr", [128, NCOL], F32)
        sq_scr = sbT(top, "sq_scr", [128, C], F32)

        ln_sems = [sem(f"s_ln{i}") for i in range(LNP)]

        def lnsem(col):
            return ln_sems[col % LNP], 8 * (col // LNP)

        def ln_v_stats(v, col, src_ap, F):
            s, base = lnsem(col)
            cs = slice(col, col + 1)
            v.reduce_sum(st_sum[:, cs], src_ap, AX.X).then_inc(s, 1)
            v.wait_ge(s, base + 2)
            v.tensor_scalar(st_m[:, cs], st_sum[:, cs], 1.0 / F, None, ALU.mult).then_inc(s, 1)
            v.tensor_scalar(st_v0[:, cs], st_sq[:, cs], 1.0 / F, None, ALU.mult).then_inc(s, 1)
            v.wait_ge(s, base + 4)
            v.tensor_scalar(st_nv[:, cs], st_m[:, cs], st_m[:, cs], st_v0[:, cs],
                            ALU.mult, ALU.subtract).then_inc(s, 1)

        def ln_v_nmr(v, col):
            s, base = lnsem(col)
            cs = slice(col, col + 1)
            v.wait_ge(s, base + 7)
            v.scalar_tensor_tensor(st_nmr[:, cs], st_m[:, cs], -1.0, st_r[:, cs],
                                   ALU.mult, ALU.mult).then_inc(s, 1)

        def ln_v_norm(v, col, out_ap, src_ap):
            s, base = lnsem(col)
            cs = slice(col, col + 1)
            v.wait_ge(s, base + 8)
            return v.tensor_scalar(out_ap, src_ap, st_r[:, cs], st_nmr[:, cs],
                                   ALU.mult, ALU.add)

        def ln_a_chain(a, col, src_ap):
            s, base = lnsem(col)
            cs = slice(col, col + 1)
            a.activation(sq_scr[:], src_ap, AF.Square, accum_out=st_sq[:, cs]).then_inc(s, 1)
            a.wait_ge(s, base + 5)
            a.activation(st_ln[:, cs], st_nv[:, cs], AF.Ln,
                         bias=c_eps[:], scale=c_m1[:]).then_inc(s, 1)
            a.wait_ge(s, base + 6)
            a.activation(st_r[:, cs], st_ln[:, cs], AF.Exp,
                         bias=c_zero[:], scale=c_mh[:]).then_inc(s, 1)

        p0 = contextlib.ExitStack()
        b_f32 = sbT(p0, "b_f32", [1, D], F32)
        s_prep = sem("s_prep")
        s_prepb = sem("s_prepb")
        with nc.Block() as blk:
            @blk.gpsimd
            def _(g):
                g.memset(c_m1[:], -1.0)
                g.memset(c_75[:], 7.5)
                g.memset(c_eps[:], 1e-5)
                g.memset(c_mh[:], -0.5)
                g.memset(c_zero[:], 0.0)
                g.memset(ones_bf[:], 1.0)
                g.dma_start(Wbf_in[:, :], W_sl[:, :]).then_inc(s_prep, 16)

            @blk.sync
            def _(s):
                s.dma_start(b_f32[:, :], b_in[:, :]).then_inc(s_prepb, 16)

            @blk.vector
            def _(v):
                v.wait_ge(s_prep, 16)
                v.wait_ge(s_prepb, 16)
                v.tensor_copy(b_bf[:], b_f32[:])

        p0.close()

        if stop_after == "P0":
            return nc
        units = []
        for nt in range(cfg.Ns // 128):
            units.append((gW_sl, gS_sl, gbf_in, nt, 0))
        for nt in range(cfg.Ns // 128):
            units.append((uW_sl, uS_sl, ubf_in, nt, 0))
        for nt in range(cfg.Cs // 128):
            for jc in range((N // 2) // JU):
                units.append((dW_sl, dS_sl, dbf_in, nt, jc))
        NU = len(units)

        dst_ = contextlib.ExitStack()
        p8 = [sbT(dst_, f"p8_{i}", [128, JU], U8) for i in range(2)]
        tsc = [sbT(dst_, f"tsc_{i}", [128, BU], F32) for i in range(2)]
        c8 = sbT(dst_, "c8", [128, C], U8)
        cf = [sbT(dst_, f"cf_{i}", [128, C], F32) for i in range(2)]
        sgn = [sbT(dst_, f"sgn_{i}", [128, C], F32) for i in range(2)]
        dxv = sbT(dst_, "dxv", [128, C], F32)
        dt1 = sbT(dst_, "dt1", [128, C], F32)
        dt3 = sbT(dst_, "dt3", [128, C], F32)
        dmag = sbT(dst_, "dmag", [128, C], F32)
        wout = [sbT(dst_, f"wout_{i}", [128, C], BF16) for i in range(2)]

        s_pk = [sem("s_pk0"), sem("s_pk1")]
        s_sc = [sem("s_sc0"), sem("s_sc1")]
        s_nib = sem("s_nib")
        s_sgn = sem("s_sgn")
        s_dec = sem("s_dec")
        s_wo = sem("s_wo")

        with nc.Block() as blk:
            @blk.gpsimd
            def _(g):
                def emit_out(i):
                    _, _, dst, nt, jc = units[i]
                    g.wait_ge(s_dec, i + 1)
                    g.dma_start(dst[nt * 128:(nt + 1) * 128, jc * C:(jc + 1) * C],
                                wout[i % 2][:]).then_inc(s_wo, 16)
                for i, (src, ssrc, dst, nt, jc) in enumerate(units):
                    if i >= 2:
                        g.wait_ge(s_nib, i - 1)
                    g.dma_start(p8[i % 2][:],
                                src[nt * 128:(nt + 1) * 128, jc * JU:(jc + 1) * JU]
                                ).then_inc(s_pk[i % 2], 16)
                    if i >= 1:
                        emit_out(i - 1)
                emit_out(NU - 1)
                g.wait_ge(s_wo, 16 * NU)

            @blk.sync
            def _(s):
                for i, (src, ssrc, dst, nt, jc) in enumerate(units):
                    if i >= 2:
                        s.wait_ge(s_dec, i - 1)
                    s.dma_start(tsc[i % 2][:],
                                ssrc[nt * 128:(nt + 1) * 128, jc * BU:(jc + 1) * BU]
                                ).then_inc(s_sc[i % 2], 16)

            @blk.vector
            def _(v):
                for i in range(NU):
                    pb, cb, sg = p8[i % 2], cf[i % 2], sgn[i % 2]
                    v.wait_ge(s_pk[i % 2], 16 * (i // 2 + 1))
                    if i >= 2:
                        v.wait_ge(s_sgn, i - 1)
                    cv = c8.ap().rearrange("a (j two) -> a j two", two=2)
                    v.tensor_scalar(cv[:, :, 0:1], pb.ap().unsqueeze(2), 15, None,
                                    ALU.bitwise_and)
                    v.tensor_scalar(cv[:, :, 1:2], pb.ap().unsqueeze(2), 4, None,
                                    ALU.logical_shift_right)
                    v.tensor_copy(cb[:], c8[:]).then_inc(s_nib, 1)
                    v.wait_ge(s_sgn, i + 1)
                    v.scalar_tensor_tensor(dxv[:], sg[:], 4.0, cb[:], ALU.mult, ALU.add)
                    v.tensor_scalar(dt1[:], dxv[:], 0.5, -2.0, ALU.mult, ALU.add)
                    v.scalar_tensor_tensor(dt1[:], dxv[:], 6.0, dt1[:], ALU.subtract, ALU.max)
                    v.tensor_scalar(dt3[:], dxv[:], 2.0, -16.0, ALU.mult, ALU.add)
                    v.tensor_tensor(dmag[:], dt1[:], dt3[:], ALU.max)
                    v.tensor_tensor(dmag[:], dmag[:], sg[:], ALU.mult)
                    if i >= 2:
                        v.wait_ge(s_wo, 16 * i)
                    v.wait_ge(s_sc[i % 2], 16 * (i // 2 + 1))
                    vv = dmag.ap().rearrange("a (b k) -> a b k", k=32)
                    wv = wout[i % 2].ap().rearrange("a (b k) -> a b k", k=32)
                    scb = tsc[i % 2].ap().unsqueeze(2).broadcast_to([128, BU, 32])
                    v.tensor_tensor(wv, vv, scb, ALU.mult).then_inc(s_dec, 1)

            @blk.scalar
            def _(a):
                load_act_set(nc, a, SET_LN)
                for i in range(NU):
                    a.wait_ge(s_nib, i + 1)
                    if i >= 2:
                        a.wait_ge(s_dec, i - 1)
                    a.activation(sgn[i % 2][:], cf[i % 2][:], AF.Sign,
                                 bias=c_75[:], scale=c_m1[:]).then_inc(s_sgn, 1)

        dst_.close()

        if stop_after == "P1":
            return nc
        s_cc = sem("s_cc")
        with nc.Block() as blk:
            @blk.gpsimd
            def _(g):
                g.wait_ge(s_wo, 16 * NU)
                g.wait_ge(s_prep, 16)
                for src, dst in ((gbf_in, gbf), (ubf_in, ubf), (dbf_in, dbf), (Wbf_in, Wbf)):
                    g.collective_compute("AllGather", ALU.bypass,
                                         replica_groups=[list(range(NCORES))],
                                         ins=[src.ap().opt()], outs=[dst.ap().opt()]
                                         ).then_inc(s_cc, 1)
                g.wait_ge(s_cc, 4)

        if stop_after == "P2":
            return nc
        pA = contextlib.ExitStack()
        xbuf = [sbT(pA, f"xbuf_{i}", [128, C], F32) for i in range(2)]
        lnbuf = [sbT(pA, f"lnbuf_{i}", [128, C], BF16) for i in range(2)]
        s_xl = [sem("s_xl0"), sem("s_xl1")]
        s_n1 = sem("s_n1")
        s_o1 = sem("s_o1")
        with nc.Block() as blk:
            @blk.sync
            def _(s):
                for i in range(TT):
                    if i >= 2:
                        s.wait_ge(s_n1, i - 1)
                    s.dma_start(xbuf[i % 2][:], x_in[i * 128:(i + 1) * 128, :]
                                ).then_inc(s_xl[i % 2], 16)

            @blk.vector
            def _(v):
                for i in range(TT):
                    v.wait_ge(s_xl[i % 2], 16 * (i // 2 + 1))
                    ln_v_stats(v, i, xbuf[i % 2][:, :], C)
                    ln_v_nmr(v, i)
                    if i >= 2:
                        v.wait_ge(s_o1, 16 * i)
                    ln_v_norm(v, i, lnbuf[i % 2][:], xbuf[i % 2][:]).then_inc(s_n1, 1)

            @blk.scalar
            def _(a):
                load_act_set(nc, a, SET_LN)
                for i in range(TT):
                    a.wait_ge(s_xl[i % 2], 16 * (i // 2 + 1))
                    ln_a_chain(a, i, xbuf[i % 2][:, :])

            @blk.gpsimd
            def _(g):
                for i in range(TT):
                    g.wait_ge(s_n1, i + 1)
                    g.dma_start(ln1d[i * 128:(i + 1) * 128, :], lnbuf[i % 2][:]
                                ).then_inc(s_o1, 16)
                g.wait_ge(s_o1, 16 * TT)

        if stop_after == "A1":
            return nc
        pT = contextlib.ExitStack()
        actT = pT.enter_context(nc.sbuf_tensor("actT", [128, CT * T], BF16, side="right"))
        s_t1 = sem("s_t1")
        with nc.Block() as blk:
            @blk.sync
            def _(s):
                for j in range(CT):
                    s.dma_start(actT[:, j * T:(j + 1) * T],
                                ln1d[:, j * 128:(j + 1) * 128], transpose=True
                                ).then_inc(s_t1, 16)

        if stop_after == "A2":
            return nc
        a3 = contextlib.ExitStack()
        wtg = [[sbT(a3, f"wtg_{p}_{j}", [128, 128], BF16) for j in range(CT)] for p in range(2)]
        wtv = [[sbT(a3, f"wtv_{p}_{j}", [128, 128], BF16) for j in range(CT)] for p in range(2)]
        gsb = [sbT(a3, f"gsb_{p}", [128, T], F32) for p in range(2)]
        vsb = [sbT(a3, f"vsb_{p}", [128, T], F32) for p in range(2)]
        hsb = sbT(a3, "hsb", [128, T], F32)
        hib = [sbT(a3, f"hib_{p}", [128, T], BF16) for p in range(2)]
        hif = sbT(a3, "hif", [128, T], F32)
        hlb = [sbT(a3, f"hlb_{p}", [128, T], BF16) for p in range(2)]
        ps_g = psT(a3, "ps_g", [128, T])
        ps_v = psT(a3, "ps_v", [128, T])

        s_wt = [sem("s_wt0"), sem("s_wt1")]
        s_mm = sem("s_mm")
        s_ev = sem("s_ev")
        s_hv = sem("s_hv")
        s_hs = sem("s_hs")

        with nc.Block() as blk:
            @blk.sync
            def _(s):
                for k in range(CT):
                    if k >= 2:
                        s.wait_ge(s_mm, 2 * (k - 1))
                    for j in range(CT):
                        s.dma_start(wtg[k % 2][j][:],
                                    Wbf[k * 128:(k + 1) * 128, j * 128:(j + 1) * 128],
                                    transpose=True).then_inc(s_wt[k % 2], 16)
                        s.dma_start(wtv[k % 2][j][:],
                                    Wbf[C + k * 128:C + (k + 1) * 128, j * 128:(j + 1) * 128],
                                    transpose=True).then_inc(s_wt[k % 2], 16)

            @blk.tensor
            def _(t):
                t.wait_ge(s_t1, 16 * CT)
                for k in range(CT):
                    t.wait_ge(s_wt[k % 2], 32 * CT * (k // 2 + 1))
                    if k >= 1:
                        t.wait_ge(s_ev, 2 * k)
                    for psum, wtiles, boff in ((ps_g, wtg[k % 2], 0), (ps_v, wtv[k % 2], C)):
                        for j in range(CT):
                            for tc in range(TC):
                                t.matmul(psum[:, tc * TCW:(tc + 1) * TCW], wtiles[j][:],
                                         actT[:, j * T + tc * TCW: j * T + (tc + 1) * TCW],
                                         start=(j == 0), stop=False)
                        mm = None
                        for tc in range(TC):
                            mm = t.matmul(psum[:, tc * TCW:(tc + 1) * TCW],
                                          b_bf[0:1, boff + k * 128: boff + (k + 1) * 128],
                                          ones_bf[0:1, 0:TCW], start=False, stop=True)
                        mm.then_inc(s_mm, 1)

            @blk.scalar
            def _(a):
                load_act_set(nc, a, SET_SIG)
                for k in range(CT):
                    a.wait_ge(s_mm, 2 * k + 1)
                    if k >= 2:
                        a.wait_ge(s_hv, 2 * (k - 1))
                    a.activation(gsb[k % 2][:], ps_g[:], AF.Sigmoid).then_inc(s_ev, 1)
                    a.wait_ge(s_mm, 2 * k + 2)
                    a.activation(vsb[k % 2][:], ps_v[:], AF.Tanh).then_inc(s_ev, 1)

            @blk.vector
            def _(v):
                for k in range(CT):
                    v.wait_ge(s_ev, 2 * (k + 1))
                    if k >= 2:
                        v.wait_ge(s_hs, 32 * k)
                    v.tensor_tensor_scan(hsb[:], gsb[k % 2][:], vsb[k % 2][:],
                                         0.0, ALU.mult, ALU.add)
                    v.tensor_copy(hib[k % 2][:], hsb[:])
                    v.tensor_copy(hif[:], hib[k % 2][:]).then_inc(s_hv, 1)
                    v.tensor_tensor(hlb[k % 2][:], hsb[:], hif[:],
                                    ALU.subtract).then_inc(s_hv, 1)

            @blk.gpsimd
            def _(g):
                for k in range(CT):
                    g.wait_ge(s_hv, 2 * (k + 1))
                    g.dma_start(hhi[k * 128:(k + 1) * 128, :], hib[k % 2][:]).then_inc(s_hs, 16)
                    g.dma_start(hlo[k * 128:(k + 1) * 128, :], hlb[k % 2][:]).then_inc(s_hs, 16)
                g.wait_ge(s_hs, 32 * CT)

        a3.close()

        if stop_after == "A3":
            return nc
        a4 = contextlib.ExitStack()
        htb = [sbT(a4, f"htb_{p}", [128, C], BF16) for p in range(2)]
        ltb = [sbT(a4, f"ltb_{p}", [128, C], BF16) for p in range(2)]
        hrow = [sbT(a4, f"hrow_{p}", [128, C], F32) for p in range(2)]
        alnb = sbT(a4, "alnb", [128, C], F32)
        x2b = [sbT(a4, f"x2b_{p}", [128, C], F32) for p in range(2)]
        s_ht = [sem("s_ht0"), sem("s_ht1")]
        s_xl2 = [sem("s_xl20"), sem("s_xl21")]
        s_hr = sem("s_hr")
        s_n2 = sem("s_n2")
        s_o2 = sem("s_o2")

        with nc.Block() as blk:
            @blk.sync
            def _(s):
                for i in range(TT):
                    if i >= 2:
                        s.wait_ge(s_n2, 2 * (i - 1))
                    s.dma_start(htb[i % 2][:], hhi[:, i * 128:(i + 1) * 128],
                                transpose=True).then_inc(s_ht[i % 2], 16)
                    s.dma_start(ltb[i % 2][:], hlo[:, i * 128:(i + 1) * 128],
                                transpose=True).then_inc(s_ht[i % 2], 16)
                    s.dma_start(xbuf[i % 2][:], x_in[i * 128:(i + 1) * 128, :]
                                ).then_inc(s_xl2[i % 2], 16)

            @blk.vector
            def _(v):
                for i in range(TT):
                    ca, cb = TT + 2 * i, TT + 2 * i + 1
                    v.wait_ge(s_ht[i % 2], 32 * (i // 2 + 1))
                    if i >= 2:
                        v.wait_ge(s_n2, 2 * (i - 1))
                    v.tensor_tensor(hrow[i % 2][:], htb[i % 2][:], ltb[i % 2][:],
                                    ALU.add).then_inc(s_hr, 1)
                    ln_v_stats(v, ca, hrow[i % 2][:, :], C)
                    ln_v_nmr(v, ca)
                    sA, baseA = lnsem(ca)
                    v.wait_ge(sA, baseA + 8)
                    v.tensor_scalar(alnb[:], hrow[i % 2][:], st_r[:, ca:ca + 1],
                                    st_nmr[:, ca:ca + 1], ALU.mult, ALU.add)
                    v.wait_ge(s_xl2[i % 2], 16 * (i // 2 + 1))
                    if i >= 2:
                        v.wait_ge(s_o2, 32 * i)
                    v.tensor_tensor(x2b[i % 2][:], alnb[:], xbuf[i % 2][:],
                                    ALU.add).then_inc(s_n2, 1)
                    ln_v_stats(v, cb, x2b[i % 2][:, :], C)
                    ln_v_nmr(v, cb)
                    ln_v_norm(v, cb, lnbuf[i % 2][:], x2b[i % 2][:]).then_inc(s_n2, 1)

            @blk.scalar
            def _(a):
                load_act_set(nc, a, SET_LN)
                for i in range(TT):
                    ca, cb = TT + 2 * i, TT + 2 * i + 1
                    a.wait_ge(s_hr, i + 1)
                    ln_a_chain(a, ca, hrow[i % 2][:, :])
                    a.wait_ge(s_n2, 2 * i + 1)
                    ln_a_chain(a, cb, x2b[i % 2][:, :])

            @blk.gpsimd
            def _(g):
                for i in range(TT):
                    g.wait_ge(s_n2, 2 * i + 1)
                    g.dma_start(x2d[i * 128:(i + 1) * 128, :], x2b[i % 2][:]).then_inc(s_o2, 16)
                    g.wait_ge(s_n2, 2 * i + 2)
                    g.dma_start(ln2d[i * 128:(i + 1) * 128, :], lnbuf[i % 2][:]
                                ).then_inc(s_o2, 16)
                g.wait_ge(s_o2, 32 * TT)

        a4.close()
        pA.close()

        if stop_after == "A4":
            return nc
        s_t2 = sem("s_t2")
        with nc.Block() as blk:
            @blk.sync
            def _(s):
                for j in range(CT):
                    s.dma_start(actT[:, j * T:(j + 1) * T],
                                ln2d[:, j * 128:(j + 1) * 128], transpose=True
                                ).then_inc(s_t2, 16)

        if stop_after == "A5":
            return nc
        b7 = contextlib.ExitStack()
        wgt = [[sbT(b7, f"wgt_{p}_{j}", [128, 512], BF16) for j in range(CT)] for p in range(2)]
        wut = [[sbT(b7, f"wut_{p}_{j}", [128, 512], BF16) for j in range(CT)] for p in range(2)]
        glb = [sbT(b7, f"glb_{p}", [128, T], F32) for p in range(2)]
        ib = [sbT(b7, f"ib_{p}", [128, T], BF16) for p in range(2)]
        ps_G = psT(b7, "ps_G", [128, T])
        ps_U = psT(b7, "ps_U", [128, T])
        s_wt2 = [sem("s_wt20"), sem("s_wt21")]
        s_gm = sem("s_gm")
        s_ge = sem("s_ge")
        s_iv = sem("s_iv")
        s_io = sem("s_io")
        NG7 = cfg.NG7

        with nc.Block() as blk:
            @blk.sync
            def _(s):
                for g7 in range(NG7):
                    if g7 >= 2:
                        s.wait_ge(s_gm, 8 * (g7 - 1))
                    r0 = g7 * 512
                    for j in range(CT):
                        s.dma_start(wgt[g7 % 2][j][:],
                                    gbf[r0:r0 + 512, j * 128:(j + 1) * 128],
                                    transpose=True).then_inc(s_wt2[g7 % 2], 16)
                        s.dma_start(wut[g7 % 2][j][:],
                                    ubf[r0:r0 + 512, j * 128:(j + 1) * 128],
                                    transpose=True).then_inc(s_wt2[g7 % 2], 16)

            @blk.tensor
            def _(t):
                t.wait_ge(s_t2, 16 * CT)
                for g7 in range(NG7):
                    t.wait_ge(s_wt2[g7 % 2], 32 * CT * (g7 // 2 + 1))
                    for nt in range(4):
                        nti = 4 * g7 + nt
                        if nti >= 1:
                            t.wait_ge(s_ge, nti)
                        mm = None
                        for j in range(CT):
                            for tc in range(TC):
                                mm = t.matmul(ps_G[:, tc * TCW:(tc + 1) * TCW],
                                              wgt[g7 % 2][j][:, nt * 128:(nt + 1) * 128],
                                              actT[:, j * T + tc * TCW: j * T + (tc + 1) * TCW],
                                              start=(j == 0), stop=(j == CT - 1))
                        mm.then_inc(s_gm, 1)
                        if nti >= 1:
                            t.wait_ge(s_iv, nti)
                        for j in range(CT):
                            for tc in range(TC):
                                mm = t.matmul(ps_U[:, tc * TCW:(tc + 1) * TCW],
                                              wut[g7 % 2][j][:, nt * 128:(nt + 1) * 128],
                                              actT[:, j * T + tc * TCW: j * T + (tc + 1) * TCW],
                                              start=(j == 0), stop=(j == CT - 1))
                        mm.then_inc(s_gm, 1)

            @blk.scalar
            def _(a):
                load_act_set(nc, a, SET_GELU)
                for nti in range(NT):
                    a.wait_ge(s_gm, 2 * nti + 1)
                    if nti >= 2:
                        a.wait_ge(s_iv, nti - 1)
                    a.activation(glb[nti % 2][:], ps_G[:], AF.Gelu_apprx_tanh
                                 ).then_inc(s_ge, 1)

            @blk.vector
            def _(v):
                for nti in range(NT):
                    v.wait_ge(s_ge, nti + 1)
                    v.wait_ge(s_gm, 2 * nti + 2)
                    if nti >= 2:
                        v.wait_ge(s_io, 16 * nti)
                    v.tensor_tensor(ib[nti % 2][:], glb[nti % 2][:], ps_U[:],
                                    ALU.mult).then_inc(s_iv, 1)

            @blk.gpsimd
            def _(g):
                for nti in range(NT):
                    g.wait_ge(s_iv, nti + 1)
                    g.dma_start(Id[nti * 128:(nti + 1) * 128, :], ib[nti % 2][:]
                                ).then_inc(s_io, 16)
                g.wait_ge(s_io, 16 * NT)

        b7.close()
        pT.close()

        if stop_after == "B7":
            return nc
        b8 = contextlib.ExitStack()
        b8d = contextlib.ExitStack()
        NT8, NG8 = cfg.NT8, cfg.NG8
        D_sb = [sbT(b8d, f"D_sb_{i}", [128, HH], F32) for i in range(TT)]
        wdt = [[sbT(b8, f"wdt_{p}_{j}", [128, HH], BF16) for j in range(NT8)] for p in range(2)]
        ig = [[sbT(b8, f"ig_{p}_{j}", [128, T], BF16) for j in range(NT8)] for p in range(2)]
        ps_d = [psT(b8, f"ps_d_{q}", [128, HH]) for q in range(2)]
        s_wt3 = [sem("s_wt30"), sem("s_wt31")]
        s_il = [sem("s_il0"), sem("s_il1")]
        s_dm = sem("s_dm")
        s_da = sem("s_da")
        s_dh = sem("s_dh")
        NIDX = 2 * NG8

        with nc.Block() as blk:
            @blk.sync
            def _(s):
                for idx in range(NIDX):
                    hh, g8 = divmod(idx, NG8)
                    if idx >= 2:
                        s.wait_ge(s_dm, TT * (idx - 1))
                    for j in range(NT8):
                        ncol = g8 * cfg.G8 + j * 128
                        s.dma_start(wdt[idx % 2][j][:],
                                    dbf[hh * HH:(hh + 1) * HH, ncol:ncol + 128],
                                    transpose=True).then_inc(s_wt3[idx % 2], 16)

            @blk.gpsimd
            def _(g):
                for idx in range(NIDX):
                    hh, g8 = divmod(idx, NG8)
                    if idx == NG8:
                        pass
                    if idx >= 2:
                        g.wait_ge(s_dm, TT * (idx - 1))
                    for j in range(NT8):
                        nrow = g8 * cfg.G8 + j * 128
                        g.dma_start(ig[idx % 2][j][:], Id[nrow:nrow + 128, :]
                                    ).then_inc(s_il[idx % 2], 16)
                    if idx == NG8 - 1:
                        g.wait_ge(s_da, TT * NG8)
                        for i in range(TT):
                            g.dma_start(dhalf[i * 128:(i + 1) * 128, :], D_sb[i][:]
                                        ).then_inc(s_dh, 16)

            @blk.tensor
            def _(t):
                for idx in range(NIDX):
                    t.wait_ge(s_wt3[idx % 2], 16 * NT8 * (idx // 2 + 1))
                    t.wait_ge(s_il[idx % 2], 16 * NT8 * (idx // 2 + 1))
                    for i in range(TT):
                        gidx = idx * TT + i
                        if gidx >= 2:
                            t.wait_ge(s_da, gidx - 1)
                        mm = None
                        for j in range(NT8):
                            for hc in range(HC):
                                mm = t.matmul(ps_d[i % 2][:, hc * HCW:(hc + 1) * HCW],
                                              ig[idx % 2][j][:, i * 128:(i + 1) * 128],
                                              wdt[idx % 2][j][:, hc * HCW:(hc + 1) * HCW],
                                              start=(j == 0), stop=(j == NT8 - 1))
                        mm.then_inc(s_dm, 1)

            @blk.vector
            def _(v):
                for idx in range(NIDX):
                    hh, g8 = divmod(idx, NG8)
                    for i in range(TT):
                        gidx = idx * TT + i
                        v.wait_ge(s_dm, gidx + 1)
                        if hh == 1 and g8 == 0 and i == 0:
                            v.wait_ge(s_dh, 16 * TT)
                        if g8 == 0:
                            v.tensor_copy(D_sb[i][:], ps_d[i % 2][:]).then_inc(s_da, 1)
                        else:
                            v.tensor_tensor(D_sb[i][:], D_sb[i][:], ps_d[i % 2][:],
                                            ALU.add).then_inc(s_da, 1)

        b8.close()

        b9 = contextlib.ExitStack()
        dh0 = [sbT(b9, f"dh0_{p}", [128, HH], F32) for p in range(2)]
        x2l = [sbT(b9, f"x2l_{p}", [128, C], F32) for p in range(2)]
        dfull = sbT(b9, "dfull", [128, C], F32)
        alnb2 = sbT(b9, "alnb2", [128, C], F32)
        obuf = [sbT(b9, f"obuf_{p}", [128, C], F32) for p in range(2)]
        s_f1 = [sem("s_f10"), sem("s_f11")]
        s_df = sem("s_df")
        s_oc = sem("s_oc")
        s_out = sem("s_out")

        with nc.Block() as blk:
            @blk.sync
            def _(s):
                for i in range(TT):
                    if i >= 2:
                        s.wait_ge(s_oc, i - 1)
                    s.dma_start(dh0[i % 2][:], dhalf[i * 128:(i + 1) * 128, :]
                                ).then_inc(s_f1[i % 2], 16)
                    s.dma_start(x2l[i % 2][:], x2d[i * 128:(i + 1) * 128, :]
                                ).then_inc(s_f1[i % 2], 16)

            @blk.vector
            def _(v):
                for i in range(TT):
                    col = 3 * TT + i
                    v.wait_ge(s_f1[i % 2], 32 * (i // 2 + 1))
                    if i >= 1:
                        sP, bP = lnsem(col - 1)
                        v.wait_ge(sP, bP + 2)
                    v.tensor_copy(dfull[:, 0:HH], dh0[i % 2][:])
                    v.tensor_copy(dfull[:, HH:C], D_sb[i][:]).then_inc(s_df, 1)
                    ln_v_stats(v, col, dfull[:, :], C)
                    ln_v_nmr(v, col)
                    ln_v_norm(v, col, alnb2[:], dfull[:, :])
                    if i >= 2:
                        v.wait_ge(s_out, 16 * i)
                    v.tensor_tensor(obuf[i % 2][:], alnb2[:], x2l[i % 2][:],
                                    ALU.add).then_inc(s_oc, 1)

            @blk.scalar
            def _(a):
                load_act_set(nc, a, SET_LN)
                for i in range(TT):
                    col = 3 * TT + i
                    a.wait_ge(s_df, i + 1)
                    ln_a_chain(a, col, dfull[:, :])

            @blk.gpsimd
            def _(g):
                for i in range(TT):
                    g.wait_ge(s_oc, i + 1)
                    g.dma_start(out[i * 128:(i + 1) * 128, :], obuf[i % 2][:]
                                ).then_inc(s_out, 16)
                g.wait_ge(s_out, 16 * TT)

        b9.close()
        b8d.close()

    return nc


def _in_maps(cfg, x, W, b, gW, gS, uW, uS, dW, dS):
    maps = []
    for r in range(NCORES):
        maps.append({
            "x": np.ascontiguousarray(x[r]),
            "W_sl": np.ascontiguousarray(W[r * cfg.Ds:(r + 1) * cfg.Ds]),
            "b": np.ascontiguousarray(b.reshape(1, -1)),
            "gW_sl": np.ascontiguousarray(gW[r * cfg.Ns:(r + 1) * cfg.Ns]),
            "gS_sl": np.ascontiguousarray(gS[r * cfg.Ns:(r + 1) * cfg.Ns]),
            "uW_sl": np.ascontiguousarray(uW[r * cfg.Ns:(r + 1) * cfg.Ns]),
            "uS_sl": np.ascontiguousarray(uS[r * cfg.Ns:(r + 1) * cfg.Ns]),
            "dW_sl": np.ascontiguousarray(dW[r * cfg.Cs:(r + 1) * cfg.Cs]),
            "dS_sl": np.ascontiguousarray(dS[r * cfg.Cs:(r + 1) * cfg.Cs]),
        })
    return maps


def kernel(L0_attn_x, L0_attn_W, L0_attn_b, L0_ffn_gate_W_packed, L0_ffn_gate_scales,
           L0_ffn_up_W_packed, L0_ffn_up_scales, L0_ffn_down_W_packed,
           L0_ffn_down_scales):
    x = np.asarray(L0_attn_x)
    B, T, C = x.shape
    N = L0_ffn_gate_W_packed.shape[0]
    cfg = Cfg(T=T, C=C, N=N)
    nc = build_program(cfg)
    maps = _in_maps(cfg, x, np.asarray(L0_attn_W), np.asarray(L0_attn_b),
                    np.asarray(L0_ffn_gate_W_packed), np.asarray(L0_ffn_gate_scales),
                    np.asarray(L0_ffn_up_W_packed), np.asarray(L0_ffn_up_scales),
                    np.asarray(L0_ffn_down_W_packed), np.asarray(L0_ffn_down_scales))
    res = run_bass_kernel_spmd(nc, maps, core_ids=list(range(NCORES)))
    return np.stack([res.results[r]["out"] for r in range(NCORES)], axis=0)


if __name__ == "__main__":
    pass


# version 02
# speedup vs baseline: 1.1869x; 1.1869x over previous
import contextlib
import numpy as np
import concourse.bass as bass
import concourse.mybir as mybir
from concourse.bass_utils import run_bass_kernel_spmd

F32 = mybir.dt.float32
BF16 = mybir.dt.bfloat16
I32 = mybir.dt.int32
U8 = mybir.dt.uint8
AF = mybir.ActivationFunctionType
ALU = mybir.AluOpType
AX = mybir.AxisListType

NCORES = 8
SET_LN = 6
SET_SIG = 2
SET_GELU = 11
LNP = 16


def load_act_set(nc, eng, idx):
    eng.add_instruction(mybir.InstLoadActFuncSet(
        name=nc.get_next_instruction_name(), act_func_set_id=idx, ins=[], outs=[]))


class Cfg:
    def __init__(self, T, C, N):
        assert T % 128 == 0 and C % 256 == 0 and N % 1024 == 0
        self.T, self.C, self.N = T, C, N
        self.D = 2 * C
        self.TT = T // 128
        self.CT = C // 128
        self.NT = N // 128
        self.TCW = min(T, 512)
        self.TC = T // self.TCW
        self.JU = C // 2
        self.BU = C // 32
        self.Ds = self.D // NCORES
        self.Ns = self.N // NCORES
        self.Cs = self.C // NCORES
        self.NG7 = N // 512
        self.HH = C // 2
        self.HCW = min(self.HH, 512)
        self.HC = self.HH // self.HCW
        self.G8 = min(N, 1024)
        self.NG8 = N // self.G8
        self.NT8 = self.G8 // 128


def build_program(cfg, stop_after=None):
    T, C, D, N = cfg.T, cfg.C, cfg.D, cfg.N
    TT, CT, NT = cfg.TT, cfg.CT, cfg.NT
    TC, TCW = cfg.TC, cfg.TCW
    HH, HC, HCW = cfg.HH, cfg.HC, cfg.HCW
    JU, BU = cfg.JU, cfg.BU
    nc = bass.Bass("TRN2", target_bir_lowering=False, detect_race_conditions=False)

    x_in = nc.dram_tensor("x", [T, C], F32, kind="ExternalInput")
    W_sl = nc.dram_tensor("W_sl", [cfg.Ds, C], F32, kind="ExternalInput")
    b_in = nc.dram_tensor("b", [1, D], F32, kind="ExternalInput")
    gW_sl = nc.dram_tensor("gW_sl", [cfg.Ns, C // 2], I32, kind="ExternalInput")
    gS_sl = nc.dram_tensor("gS_sl", [cfg.Ns, C // 32], F32, kind="ExternalInput")
    uW_sl = nc.dram_tensor("uW_sl", [cfg.Ns, C // 2], I32, kind="ExternalInput")
    uS_sl = nc.dram_tensor("uS_sl", [cfg.Ns, C // 32], F32, kind="ExternalInput")
    dW_sl = nc.dram_tensor("dW_sl", [cfg.Cs, N // 2], I32, kind="ExternalInput")
    dS_sl = nc.dram_tensor("dS_sl", [cfg.Cs, N // 32], F32, kind="ExternalInput")
    out = nc.dram_tensor("out", [T, C], F32, kind="ExternalOutput")

    Wbf_in = nc.dram_tensor("Wbf_in", [cfg.Ds, C], BF16)
    Wbf = nc.dram_tensor("Wbf", [D, C], BF16)
    gbf_in = nc.dram_tensor("gbf_in", [cfg.Ns, C], BF16)
    gbf = nc.dram_tensor("gbf", [N, C], BF16, addr_space="Shared")
    ubf_in = nc.dram_tensor("ubf_in", [cfg.Ns, C], BF16)
    ubf = nc.dram_tensor("ubf", [N, C], BF16, addr_space="Shared")
    dbf_in = nc.dram_tensor("dbf_in", [cfg.Cs, N], BF16)
    dbf = nc.dram_tensor("dbf", [C, N], BF16, addr_space="Shared")
    ln1d = nc.dram_tensor("ln1d", [T, C], BF16)
    ln2d = nc.dram_tensor("ln2d", [T, C], BF16)
    hhi = nc.dram_tensor("hhi", [C, T], BF16)
    hlo = nc.dram_tensor("hlo", [C, T], BF16)
    x2d = nc.dram_tensor("x2d", [T, C], F32)
    Id = nc.dram_tensor("Id", [N, T], BF16)
    dhalf = nc.dram_tensor("dhalf", [T, HH], F32)

    top = contextlib.ExitStack()
    sem = lambda name: top.enter_context(nc.semaphore(name))
    sbT = lambda st, name, shape, dt: st.enter_context(nc.sbuf_tensor(name, shape, dt))
    psT = lambda st, name, shape: st.enter_context(nc.psum_tensor(name, shape, F32))

    with top:
        c_m1 = sbT(top, "c_m1", [128, 1], F32)
        c_75 = sbT(top, "c_75", [128, 1], F32)
        c_eps = sbT(top, "c_eps", [128, 1], F32)
        c_mh = sbT(top, "c_mh", [128, 1], F32)
        c_zero = sbT(top, "c_zero", [128, 1], F32)
        ones_bf = sbT(top, "ones_bf", [1, 512], BF16)
        b_bf = sbT(top, "b_bf", [1, D], BF16)
        NCOL = 4 * TT
        st_sum = sbT(top, "st_sum", [128, NCOL], F32)
        st_sq = sbT(top, "st_sq", [128, NCOL], F32)
        st_m = sbT(top, "st_m", [128, NCOL], F32)
        st_v0 = sbT(top, "st_v0", [128, NCOL], F32)
        st_nv = sbT(top, "st_nv", [128, NCOL], F32)
        st_ln = sbT(top, "st_ln", [128, NCOL], F32)
        st_r = sbT(top, "st_r", [128, NCOL], F32)
        st_nmr = sbT(top, "st_nmr", [128, NCOL], F32)
        sq_scr = sbT(top, "sq_scr", [128, C], F32)

        ln_sems = [sem(f"s_ln{i}") for i in range(LNP)]

        def lnsem(col):
            return ln_sems[col % LNP], 8 * (col // LNP)

        def ln_v_stats(v, col, src_ap, F):
            s, base = lnsem(col)
            cs = slice(col, col + 1)
            v.reduce_sum(st_sum[:, cs], src_ap, AX.X).then_inc(s, 1)
            v.wait_ge(s, base + 2)
            v.tensor_scalar(st_m[:, cs], st_sum[:, cs], 1.0 / F, None, ALU.mult).then_inc(s, 1)
            v.tensor_scalar(st_v0[:, cs], st_sq[:, cs], 1.0 / F, None, ALU.mult).then_inc(s, 1)
            v.wait_ge(s, base + 4)
            v.tensor_scalar(st_nv[:, cs], st_m[:, cs], st_m[:, cs], st_v0[:, cs],
                            ALU.mult, ALU.subtract).then_inc(s, 1)

        def ln_v_nmr(v, col):
            s, base = lnsem(col)
            cs = slice(col, col + 1)
            v.wait_ge(s, base + 7)
            v.scalar_tensor_tensor(st_nmr[:, cs], st_m[:, cs], -1.0, st_r[:, cs],
                                   ALU.mult, ALU.mult).then_inc(s, 1)

        def ln_v_norm(v, col, out_ap, src_ap):
            s, base = lnsem(col)
            cs = slice(col, col + 1)
            v.wait_ge(s, base + 8)
            return v.tensor_scalar(out_ap, src_ap, st_r[:, cs], st_nmr[:, cs],
                                   ALU.mult, ALU.add)

        def ln_a_chain(a, col, src_ap):
            s, base = lnsem(col)
            cs = slice(col, col + 1)
            a.activation(sq_scr[:], src_ap, AF.Square, accum_out=st_sq[:, cs]).then_inc(s, 1)
            a.wait_ge(s, base + 5)
            a.activation(st_ln[:, cs], st_nv[:, cs], AF.Ln,
                         bias=c_eps[:], scale=c_m1[:]).then_inc(s, 1)
            a.wait_ge(s, base + 6)
            a.activation(st_r[:, cs], st_ln[:, cs], AF.Exp,
                         bias=c_zero[:], scale=c_mh[:]).then_inc(s, 1)

        units = []
        for nt in range(cfg.Ns // 128):
            units.append((gW_sl, gS_sl, gbf_in, nt, 0))
        NU_G = len(units)
        for nt in range(cfg.Ns // 128):
            units.append((uW_sl, uS_sl, ubf_in, nt, 0))
        NU_U = len(units)
        for nt in range(cfg.Cs // 128):
            for jc in range((N // 2) // JU):
                units.append((dW_sl, dS_sl, dbf_in, nt, jc))
        NU = len(units)

        bi = contextlib.ExitStack()
        p8 = [sbT(bi, f"p8_{i}", [128, JU], U8) for i in range(2)]
        scs = sbT(bi, "scs", [128, NU * BU], F32)
        c8 = sbT(bi, "c8", [128, C], U8)
        cf = sbT(bi, "cf", [128, C], BF16)
        sgn = [sbT(bi, f"sgn_{i}", [128, C], BF16) for i in range(2)]
        dxv = sbT(bi, "dxv", [128, C], BF16)
        dt1 = sbT(bi, "dt1", [128, C], BF16)
        dt3 = sbT(bi, "dt3", [128, C], BF16)
        wout = [sbT(bi, f"wout_{i}", [128, C], BF16) for i in range(2)]
        xbuf = [sbT(bi, f"xbuf_{i}", [128, C], F32) for i in range(2)]
        lnbuf = [sbT(bi, f"lnbuf_{i}", [128, C], BF16) for i in range(2)]
        wtg = [[sbT(bi, f"wtg_{p}_{j}", [128, 128], BF16) for j in range(CT)] for p in range(2)]
        wtv = [[sbT(bi, f"wtv_{p}_{j}", [128, 128], BF16) for j in range(CT)] for p in range(2)]
        gsb = sbT(bi, "gsb", [128, T], F32)
        vsb = sbT(bi, "vsb", [128, T], F32)
        hsb = sbT(bi, "hsb", [128, T], F32)
        hib = sbT(bi, "hib", [128, T], BF16)
        hlb = sbT(bi, "hlb", [128, T], BF16)
        pT = contextlib.ExitStack()
        actT = pT.enter_context(nc.sbuf_tensor("actT", [128, CT * T], BF16, side="right"))
        psA = contextlib.ExitStack()
        ps_g = psT(psA, "ps_g", [128, T])
        ps_v = psT(psA, "ps_v", [128, T])

        s_prep = sem("s_prep")
        s_cc = sem("s_cc")
        s_pk = [sem("s_pk0"), sem("s_pk1")]
        s_sc = sem("s_sc")
        s_nib = sem("s_nib")
        s_sgn = sem("s_sgn")
        s_dec = sem("s_dec")
        s_wo = sem("s_wo")
        s_xl = [sem("s_xl0"), sem("s_xl1")]
        s_n1 = sem("s_n1")
        s_o1 = sem("s_o1")
        s_t1 = sem("s_t1")
        s_wt = [sem("s_wt0"), sem("s_wt1")]
        s_mm = sem("s_mm")
        s_ev = sem("s_ev")
        s_hv = sem("s_hv")
        s_hs = sem("s_hs")
        RG = [list(range(NCORES))]

        with nc.Block() as blk:
            @blk.gpsimd
            def _(g):
                g.memset(c_m1[:], -1.0)
                g.memset(c_75[:], 7.5)
                g.memset(c_eps[:], 1e-5)
                g.memset(c_mh[:], -0.5)
                g.memset(c_zero[:], 0.0)
                g.memset(ones_bf[:], 1.0)
                g.dma_start(Wbf_in[:, :], W_sl[:, :]).then_inc(s_prep, 16)
                g.dma_start(b_bf[:, :], b_in[:, :]).then_inc(s_prep, 16)
                g.wait_ge(s_prep, 16)
                g.collective_compute("AllGather", ALU.bypass, replica_groups=RG,
                                     ins=[Wbf_in.ap().opt()], outs=[Wbf.ap().opt()]
                                     ).then_inc(s_cc, 1)

                def emit_out(i):
                    _, _, dst, nt, jc = units[i]
                    g.wait_ge(s_dec, i + 1)
                    g.dma_start(dst[nt * 128:(nt + 1) * 128, jc * C:(jc + 1) * C],
                                wout[i % 2][:]).then_inc(s_wo, 16)
                    if i + 1 == NU_G:
                        g.wait_ge(s_wo, 16 * NU_G)
                        g.collective_compute("AllGather", ALU.bypass, replica_groups=RG,
                                             ins=[gbf_in.ap().opt()], outs=[gbf.ap().opt()]
                                             ).then_inc(s_cc, 1)
                    if i + 1 == NU_U:
                        g.wait_ge(s_wo, 16 * NU_U)
                        g.collective_compute("AllGather", ALU.bypass, replica_groups=RG,
                                             ins=[ubf_in.ap().opt()], outs=[ubf.ap().opt()]
                                             ).then_inc(s_cc, 1)
                    if i + 1 == NU:
                        g.wait_ge(s_wo, 16 * NU)
                        g.collective_compute("AllGather", ALU.bypass, replica_groups=RG,
                                             ins=[dbf_in.ap().opt()], outs=[dbf.ap().opt()]
                                             ).then_inc(s_cc, 1)

                for i, (src, ssrc, dst, nt, jc) in enumerate(units):
                    if i >= 2:
                        g.wait_ge(s_nib, i - 1)
                    g.dma_start(p8[i % 2][:],
                                src[nt * 128:(nt + 1) * 128, jc * JU:(jc + 1) * JU]
                                ).then_inc(s_pk[i % 2], 16)
                    if i >= 1:
                        emit_out(i - 1)
                emit_out(NU - 1)

            @blk.sync
            def _(s):
                for i in range(TT):
                    if i >= 2:
                        s.wait_ge(s_n1, i - 1)
                    s.dma_start(xbuf[i % 2][:], x_in[i * 128:(i + 1) * 128, :]
                                ).then_inc(s_xl[i % 2], 16)
                for i, (src, ssrc, dst, nt, jc) in enumerate(units):
                    s.dma_start(scs[:, i * BU:(i + 1) * BU],
                                ssrc[nt * 128:(nt + 1) * 128, jc * BU:(jc + 1) * BU]
                                ).then_inc(s_sc, 16)
                s.wait_ge(s_cc, 1)

                def wt_pair(k):
                    if k >= 2:
                        s.wait_ge(s_mm, 2 * (k - 1))
                    for j in range(CT):
                        s.dma_start(wtg[k % 2][j][:],
                                    Wbf[k * 128:(k + 1) * 128, j * 128:(j + 1) * 128],
                                    transpose=True).then_inc(s_wt[k % 2], 16)
                        s.dma_start(wtv[k % 2][j][:],
                                    Wbf[C + k * 128:C + (k + 1) * 128, j * 128:(j + 1) * 128],
                                    transpose=True).then_inc(s_wt[k % 2], 16)

                wt_pair(0)
                wt_pair(1)
                s.wait_ge(s_o1, 16 * TT)
                for j in range(CT):
                    s.dma_start(actT[:, j * T:(j + 1) * T],
                                ln1d[:, j * 128:(j + 1) * 128], transpose=True
                                ).then_inc(s_t1, 16)
                for k in range(2, CT):
                    wt_pair(k)

            @blk.vector
            def _(v):
                def a1_tile(i):
                    v.wait_ge(s_xl[i % 2], 16 * (i // 2 + 1))
                    ln_v_stats(v, i, xbuf[i % 2][:, :], C)
                    ln_v_nmr(v, i)
                    if i >= 2:
                        v.wait_ge(s_o1, 16 * i)
                    ln_v_norm(v, i, lnbuf[i % 2][:], xbuf[i % 2][:]).then_inc(s_n1, 1)

                def dec_unit(i):
                    pb = p8[i % 2]
                    v.wait_ge(s_pk[i % 2], 16 * (i // 2 + 1))
                    if i >= 1:
                        v.wait_ge(s_sgn, i)
                    cv = c8.ap().rearrange("a (j two) -> a j two", two=2)
                    v.tensor_scalar(cv[:, :, 0:1], pb.ap().unsqueeze(2), 15, None,
                                    ALU.bitwise_and)
                    v.tensor_scalar(cv[:, :, 1:2], pb.ap().unsqueeze(2), 4, None,
                                    ALU.logical_shift_right)
                    v.tensor_copy(cf[:], c8[:]).then_inc(s_nib, 1)
                    v.wait_ge(s_sgn, i + 1)
                    sg = sgn[i % 2]
                    v.scalar_tensor_tensor(dxv[:], sg[:], 4.0, cf[:], ALU.mult, ALU.add)
                    v.tensor_scalar(dt1[:], dxv[:], 0.5, -2.0, ALU.mult, ALU.add)
                    v.scalar_tensor_tensor(dt1[:], dxv[:], 6.0, dt1[:], ALU.subtract, ALU.max)
                    v.tensor_scalar(dt3[:], dxv[:], 2.0, -16.0, ALU.mult, ALU.add)
                    v.tensor_tensor(dt3[:], dt1[:], dt3[:], ALU.max)
                    v.tensor_tensor(dt1[:], dt3[:], sg[:], ALU.mult)
                    if i >= 2:
                        v.wait_ge(s_wo, 16 * i)
                    if i == 0:
                        v.wait_ge(s_sc, 16 * NU)
                    vv = dt1.ap().rearrange("a (b k) -> a b k", k=32)
                    wv = wout[i % 2].ap().rearrange("a (b k) -> a b k", k=32)
                    scb = scs[:, i * BU:(i + 1) * BU].unsqueeze(2).broadcast_to([128, BU, 32])
                    v.tensor_tensor(wv, vv, scb, ALU.mult).then_inc(s_dec, 1)

                def scan_pair(k):
                    v.wait_ge(s_ev, 2 * (k + 1))
                    if k >= 1:
                        v.wait_ge(s_hs, 32 * k)
                    v.tensor_tensor_scan(hsb[:], gsb[:], vsb[:], 0.0, ALU.mult, ALU.add)
                    v.tensor_copy(hib[:], hsb[:])
                    v.tensor_tensor(hlb[:], hsb[:], hib[:], ALU.subtract).then_inc(s_hv, 1)

                for i in range(TT):
                    a1_tile(i)
                for i in range(NU_G):
                    dec_unit(i)
                for k in range(CT):
                    scan_pair(k)
                    u0 = NU_G + 2 * k
                    for u in range(u0, min(u0 + 2, NU)):
                        dec_unit(u)
                for u in range(NU_G + 2 * CT, NU):
                    dec_unit(u)

            @blk.scalar
            def _(a):
                load_act_set(nc, a, SET_LN)
                for i in range(TT):
                    a.wait_ge(s_xl[i % 2], 16 * (i // 2 + 1))
                    ln_a_chain(a, i, xbuf[i % 2][:, :])
                    a.wait_ge(s_n1, i + 1)
                    nc.scalar.dma_start(ln1d[i * 128:(i + 1) * 128, :], lnbuf[i % 2][:]
                                        ).then_inc(s_o1, 16)
                load_act_set(nc, a, SET_SIG)

                def sign_unit(i):
                    a.wait_ge(s_nib, i + 1)
                    if i >= 2:
                        a.wait_ge(s_dec, i - 1)
                    a.activation(sgn[i % 2][:], cf[:], AF.Sign,
                                 bias=c_75[:], scale=c_m1[:]).then_inc(s_sgn, 1)

                for i in range(NU_G):
                    sign_unit(i)
                for k in range(CT):
                    if k >= 1:
                        a.wait_ge(s_hv, k)
                        nc.scalar.dma_start(hhi[(k - 1) * 128:k * 128, :], hib[:]
                                            ).then_inc(s_hs, 16)
                        nc.scalar.dma_start(hlo[(k - 1) * 128:k * 128, :], hlb[:]
                                            ).then_inc(s_hs, 16)
                    a.wait_ge(s_mm, 2 * k + 1)
                    if k >= 1:
                        a.wait_ge(s_hv, k)
                    a.activation(gsb[:], ps_g[:], AF.Sigmoid).then_inc(s_ev, 1)
                    a.wait_ge(s_mm, 2 * k + 2)
                    a.activation(vsb[:], ps_v[:], AF.Tanh).then_inc(s_ev, 1)
                    u0 = NU_G + 2 * k
                    for u in range(u0, min(u0 + 2, NU)):
                        sign_unit(u)
                a.wait_ge(s_hv, CT)
                nc.scalar.dma_start(hhi[(CT - 1) * 128:CT * 128, :], hib[:]).then_inc(s_hs, 16)
                nc.scalar.dma_start(hlo[(CT - 1) * 128:CT * 128, :], hlb[:]).then_inc(s_hs, 16)
                for u in range(NU_G + 2 * CT, NU):
                    sign_unit(u)

            @blk.tensor
            def _(t):
                t.wait_ge(s_t1, 16 * CT)
                t.wait_ge(s_prep, 32)
                for k in range(CT):
                    t.wait_ge(s_wt[k % 2], 32 * CT * (k // 2 + 1))
                    if k >= 1:
                        t.wait_ge(s_ev, 2 * k)
                    for psum, wtiles, boff in ((ps_g, wtg[k % 2], 0), (ps_v, wtv[k % 2], C)):
                        for j in range(CT):
                            for tc in range(TC):
                                t.matmul(psum[:, tc * TCW:(tc + 1) * TCW], wtiles[j][:],
                                         actT[:, j * T + tc * TCW: j * T + (tc + 1) * TCW],
                                         start=(j == 0), stop=False)
                        mm = None
                        for tc in range(TC):
                            mm = t.matmul(psum[:, tc * TCW:(tc + 1) * TCW],
                                          b_bf[0:1, boff + k * 128: boff + (k + 1) * 128],
                                          ones_bf[0:1, 0:TCW], start=False, stop=True)
                        mm.then_inc(s_mm, 1)

        bi.close()
        psA.close()

        a4 = contextlib.ExitStack()
        xbuf2 = [sbT(a4, f"xbuf2_{p}", [128, C], F32) for p in range(2)]
        lnbuf2 = [sbT(a4, f"lnbuf2_{p}", [128, C], BF16) for p in range(2)]
        htb = [sbT(a4, f"htb_{p}", [128, C], BF16) for p in range(2)]
        ltb = [sbT(a4, f"ltb_{p}", [128, C], BF16) for p in range(2)]
        hrow = [sbT(a4, f"hrow_{p}", [128, C], F32) for p in range(2)]
        alnb = sbT(a4, "alnb", [128, C], F32)
        x2b = [sbT(a4, f"x2b_{p}", [128, C], F32) for p in range(2)]
        s_ht = [sem("s_ht0"), sem("s_ht1")]
        s_xl2 = [sem("s_xl20"), sem("s_xl21")]
        s_hr = sem("s_hr")
        s_n2 = sem("s_n2")
        s_o2 = sem("s_o2")
        s_t2 = sem("s_t2")

        with nc.Block() as blk:
            @blk.sync
            def _(s):
                s.wait_ge(s_hs, 32 * CT)
                for i in range(TT):
                    if i >= 2:
                        s.wait_ge(s_n2, 2 * (i - 1))
                    s.dma_start(htb[i % 2][:], hhi[:, i * 128:(i + 1) * 128],
                                transpose=True).then_inc(s_ht[i % 2], 16)
                    s.dma_start(ltb[i % 2][:], hlo[:, i * 128:(i + 1) * 128],
                                transpose=True).then_inc(s_ht[i % 2], 16)
                    s.dma_start(xbuf2[i % 2][:], x_in[i * 128:(i + 1) * 128, :]
                                ).then_inc(s_xl2[i % 2], 16)
                s.wait_ge(s_o2, 32 * TT)
                for j in range(CT):
                    s.dma_start(actT[:, j * T:(j + 1) * T],
                                ln2d[:, j * 128:(j + 1) * 128], transpose=True
                                ).then_inc(s_t2, 16)

            @blk.vector
            def _(v):
                for i in range(TT):
                    ca, cb_ = TT + 2 * i, TT + 2 * i + 1
                    v.wait_ge(s_ht[i % 2], 32 * (i // 2 + 1))
                    if i >= 2:
                        v.wait_ge(s_n2, 2 * (i - 1))
                    v.tensor_tensor(hrow[i % 2][:], htb[i % 2][:], ltb[i % 2][:],
                                    ALU.add).then_inc(s_hr, 1)
                    ln_v_stats(v, ca, hrow[i % 2][:, :], C)
                    ln_v_nmr(v, ca)
                    sA, baseA = lnsem(ca)
                    v.wait_ge(sA, baseA + 8)
                    v.tensor_scalar(alnb[:], hrow[i % 2][:], st_r[:, ca:ca + 1],
                                    st_nmr[:, ca:ca + 1], ALU.mult, ALU.add)
                    v.wait_ge(s_xl2[i % 2], 16 * (i // 2 + 1))
                    if i >= 2:
                        v.wait_ge(s_o2, 32 * i)
                    v.tensor_tensor(x2b[i % 2][:], alnb[:], xbuf2[i % 2][:],
                                    ALU.add).then_inc(s_n2, 1)
                    ln_v_stats(v, cb_, x2b[i % 2][:, :], C)
                    ln_v_nmr(v, cb_)
                    ln_v_norm(v, cb_, lnbuf2[i % 2][:], x2b[i % 2][:]).then_inc(s_n2, 1)

            @blk.scalar
            def _(a):
                load_act_set(nc, a, SET_LN)
                for i in range(TT):
                    ca, cb_ = TT + 2 * i, TT + 2 * i + 1
                    a.wait_ge(s_hr, i + 1)
                    ln_a_chain(a, ca, hrow[i % 2][:, :])
                    a.wait_ge(s_n2, 2 * i + 1)
                    ln_a_chain(a, cb_, x2b[i % 2][:, :])

            @blk.gpsimd
            def _(g):
                for i in range(TT):
                    g.wait_ge(s_n2, 2 * i + 1)
                    g.dma_start(x2d[i * 128:(i + 1) * 128, :], x2b[i % 2][:]).then_inc(s_o2, 16)
                    g.wait_ge(s_n2, 2 * i + 2)
                    g.dma_start(ln2d[i * 128:(i + 1) * 128, :], lnbuf2[i % 2][:]
                                ).then_inc(s_o2, 16)

        a4.close()

        if stop_after == "BI":
            return nc

        b7 = contextlib.ExitStack()
        wgt = [[sbT(b7, f"wgt_{p}_{j}", [128, 512], BF16) for j in range(CT)] for p in range(2)]
        wut = [[sbT(b7, f"wut_{p}_{j}", [128, 512], BF16) for j in range(CT)] for p in range(2)]
        glb = [sbT(b7, f"glb_{p}", [128, T], F32) for p in range(2)]
        ib = [sbT(b7, f"ib_{p}", [128, T], BF16) for p in range(2)]
        ps_G = psT(b7, "ps_G", [128, T])
        ps_U = psT(b7, "ps_U", [128, T])
        s_wt2 = [sem("s_wt20"), sem("s_wt21")]
        s_gm = sem("s_gm")
        s_ge = sem("s_ge")
        s_iv = sem("s_iv")
        s_io = sem("s_io")
        NG7 = cfg.NG7

        with nc.Block() as blk:
            @blk.sync
            def _(s):
                s.wait_ge(s_cc, 3)
                for g7 in range(NG7):
                    if g7 >= 2:
                        s.wait_ge(s_gm, 8 * (g7 - 1))
                    r0 = g7 * 512
                    for j in range(CT):
                        s.dma_start(wgt[g7 % 2][j][:],
                                    gbf[r0:r0 + 512, j * 128:(j + 1) * 128],
                                    transpose=True).then_inc(s_wt2[g7 % 2], 16)
                        s.dma_start(wut[g7 % 2][j][:],
                                    ubf[r0:r0 + 512, j * 128:(j + 1) * 128],
                                    transpose=True).then_inc(s_wt2[g7 % 2], 16)

            @blk.tensor
            def _(t):
                t.wait_ge(s_t2, 16 * CT)
                for g7 in range(NG7):
                    t.wait_ge(s_wt2[g7 % 2], 32 * CT * (g7 // 2 + 1))
                    for nt in range(4):
                        nti = 4 * g7 + nt
                        if nti >= 1:
                            t.wait_ge(s_ge, nti)
                        mm = None
                        for j in range(CT):
                            for tc in range(TC):
                                mm = t.matmul(ps_G[:, tc * TCW:(tc + 1) * TCW],
                                              wgt[g7 % 2][j][:, nt * 128:(nt + 1) * 128],
                                              actT[:, j * T + tc * TCW: j * T + (tc + 1) * TCW],
                                              start=(j == 0), stop=(j == CT - 1))
                        mm.then_inc(s_gm, 1)
                        if nti >= 1:
                            t.wait_ge(s_iv, nti)
                        for j in range(CT):
                            for tc in range(TC):
                                mm = t.matmul(ps_U[:, tc * TCW:(tc + 1) * TCW],
                                              wut[g7 % 2][j][:, nt * 128:(nt + 1) * 128],
                                              actT[:, j * T + tc * TCW: j * T + (tc + 1) * TCW],
                                              start=(j == 0), stop=(j == CT - 1))
                        mm.then_inc(s_gm, 1)

            @blk.scalar
            def _(a):
                load_act_set(nc, a, SET_GELU)
                for nti in range(NT):
                    a.wait_ge(s_gm, 2 * nti + 1)
                    if nti >= 2:
                        a.wait_ge(s_iv, nti - 1)
                    a.activation(glb[nti % 2][:], ps_G[:], AF.Gelu_apprx_tanh
                                 ).then_inc(s_ge, 1)

            @blk.vector
            def _(v):
                for nti in range(NT):
                    v.wait_ge(s_ge, nti + 1)
                    v.wait_ge(s_gm, 2 * nti + 2)
                    if nti >= 2:
                        v.wait_ge(s_io, 16 * nti)
                    v.tensor_tensor(ib[nti % 2][:], glb[nti % 2][:], ps_U[:],
                                    ALU.mult).then_inc(s_iv, 1)

            @blk.gpsimd
            def _(g):
                for nti in range(NT):
                    g.wait_ge(s_iv, nti + 1)
                    g.dma_start(Id[nti * 128:(nti + 1) * 128, :], ib[nti % 2][:]
                                ).then_inc(s_io, 16)
                g.wait_ge(s_io, 16 * NT)

        b7.close()
        pT.close()

        if stop_after == "B7":
            return nc

        b8 = contextlib.ExitStack()
        b8d = contextlib.ExitStack()
        NT8, NG8 = cfg.NT8, cfg.NG8
        D_sb = [sbT(b8d, f"D_sb_{i}", [128, HH], F32) for i in range(TT)]
        wdt = [[sbT(b8, f"wdt_{p}_{j}", [128, HH], BF16) for j in range(NT8)] for p in range(2)]
        ig = [[sbT(b8, f"ig_{p}_{j}", [128, T], BF16) for j in range(NT8)] for p in range(2)]
        ps_d = [psT(b8, f"ps_d_{q}", [128, HH]) for q in range(2)]
        s_wt3 = [sem("s_wt30"), sem("s_wt31")]
        s_il = [sem("s_il0"), sem("s_il1")]
        s_dm = sem("s_dm")
        s_da = sem("s_da")
        s_dh = sem("s_dh")
        NIDX = 2 * NG8

        with nc.Block() as blk:
            @blk.sync
            def _(s):
                s.wait_ge(s_cc, 4)
                for idx in range(NIDX):
                    hh, g8 = divmod(idx, NG8)
                    if idx >= 2:
                        s.wait_ge(s_dm, TT * (idx - 1))
                    for j in range(NT8):
                        ncol = g8 * cfg.G8 + j * 128
                        s.dma_start(wdt[idx % 2][j][:],
                                    dbf[hh * HH:(hh + 1) * HH, ncol:ncol + 128],
                                    transpose=True).then_inc(s_wt3[idx % 2], 16)

            @blk.gpsimd
            def _(g):
                for idx in range(NIDX):
                    hh, g8 = divmod(idx, NG8)
                    if idx >= 2:
                        g.wait_ge(s_dm, TT * (idx - 1))
                    for j in range(NT8):
                        nrow = g8 * cfg.G8 + j * 128
                        g.dma_start(ig[idx % 2][j][:], Id[nrow:nrow + 128, :]
                                    ).then_inc(s_il[idx % 2], 16)
                    if idx == NG8 - 1:
                        g.wait_ge(s_da, TT * NG8)
                        for i in range(TT):
                            g.dma_start(dhalf[i * 128:(i + 1) * 128, :], D_sb[i][:]
                                        ).then_inc(s_dh, 16)

            @blk.tensor
            def _(t):
                for idx in range(NIDX):
                    t.wait_ge(s_wt3[idx % 2], 16 * NT8 * (idx // 2 + 1))
                    t.wait_ge(s_il[idx % 2], 16 * NT8 * (idx // 2 + 1))
                    for i in range(TT):
                        gidx = idx * TT + i
                        if gidx >= 2:
                            t.wait_ge(s_da, gidx - 1)
                        mm = None
                        for j in range(NT8):
                            for hc in range(HC):
                                mm = t.matmul(ps_d[i % 2][:, hc * HCW:(hc + 1) * HCW],
                                              ig[idx % 2][j][:, i * 128:(i + 1) * 128],
                                              wdt[idx % 2][j][:, hc * HCW:(hc + 1) * HCW],
                                              start=(j == 0), stop=(j == NT8 - 1))
                        mm.then_inc(s_dm, 1)

            @blk.vector
            def _(v):
                for idx in range(NIDX):
                    hh, g8 = divmod(idx, NG8)
                    for i in range(TT):
                        gidx = idx * TT + i
                        v.wait_ge(s_dm, gidx + 1)
                        if hh == 1 and g8 == 0 and i == 0:
                            v.wait_ge(s_dh, 16 * TT)
                        if g8 == 0:
                            v.tensor_copy(D_sb[i][:], ps_d[i % 2][:]).then_inc(s_da, 1)
                        else:
                            v.tensor_tensor(D_sb[i][:], D_sb[i][:], ps_d[i % 2][:],
                                            ALU.add).then_inc(s_da, 1)

        b8.close()

        b9 = contextlib.ExitStack()
        dh0 = [sbT(b9, f"dh0_{p}", [128, HH], F32) for p in range(2)]
        x2l = [sbT(b9, f"x2l_{p}", [128, C], F32) for p in range(2)]
        dfull = sbT(b9, "dfull", [128, C], F32)
        alnb2 = sbT(b9, "alnb2", [128, C], F32)
        obuf = [sbT(b9, f"obuf_{p}", [128, C], F32) for p in range(2)]
        s_f1 = [sem("s_f10"), sem("s_f11")]
        s_df = sem("s_df")
        s_oc = sem("s_oc")
        s_out = sem("s_out")

        with nc.Block() as blk:
            @blk.sync
            def _(s):
                for i in range(TT):
                    if i >= 2:
                        s.wait_ge(s_oc, i - 1)
                    s.dma_start(dh0[i % 2][:], dhalf[i * 128:(i + 1) * 128, :]
                                ).then_inc(s_f1[i % 2], 16)
                    s.dma_start(x2l[i % 2][:], x2d[i * 128:(i + 1) * 128, :]
                                ).then_inc(s_f1[i % 2], 16)

            @blk.vector
            def _(v):
                for i in range(TT):
                    col = 3 * TT + i
                    v.wait_ge(s_f1[i % 2], 32 * (i // 2 + 1))
                    if i >= 1:
                        sP, bP = lnsem(col - 1)
                        v.wait_ge(sP, bP + 2)
                    v.tensor_copy(dfull[:, 0:HH], dh0[i % 2][:])
                    v.tensor_copy(dfull[:, HH:C], D_sb[i][:]).then_inc(s_df, 1)
                    ln_v_stats(v, col, dfull[:, :], C)
                    ln_v_nmr(v, col)
                    ln_v_norm(v, col, alnb2[:], dfull[:, :])
                    if i >= 2:
                        v.wait_ge(s_out, 16 * i)
                    v.tensor_tensor(obuf[i % 2][:], alnb2[:], x2l[i % 2][:],
                                    ALU.add).then_inc(s_oc, 1)

            @blk.scalar
            def _(a):
                load_act_set(nc, a, SET_LN)
                for i in range(TT):
                    col = 3 * TT + i
                    a.wait_ge(s_df, i + 1)
                    ln_a_chain(a, col, dfull[:, :])

            @blk.gpsimd
            def _(g):
                for i in range(TT):
                    g.wait_ge(s_oc, i + 1)
                    g.dma_start(out[i * 128:(i + 1) * 128, :], obuf[i % 2][:]
                                ).then_inc(s_out, 16)
                g.wait_ge(s_out, 16 * TT)

        b9.close()
        b8d.close()

    return nc


def _in_maps(cfg, x, W, b, gW, gS, uW, uS, dW, dS):
    maps = []
    for r in range(NCORES):
        maps.append({
            "x": np.ascontiguousarray(x[r]),
            "W_sl": np.ascontiguousarray(W[r * cfg.Ds:(r + 1) * cfg.Ds]),
            "b": np.ascontiguousarray(b.reshape(1, -1)),
            "gW_sl": np.ascontiguousarray(gW[r * cfg.Ns:(r + 1) * cfg.Ns]),
            "gS_sl": np.ascontiguousarray(gS[r * cfg.Ns:(r + 1) * cfg.Ns]),
            "uW_sl": np.ascontiguousarray(uW[r * cfg.Ns:(r + 1) * cfg.Ns]),
            "uS_sl": np.ascontiguousarray(uS[r * cfg.Ns:(r + 1) * cfg.Ns]),
            "dW_sl": np.ascontiguousarray(dW[r * cfg.Cs:(r + 1) * cfg.Cs]),
            "dS_sl": np.ascontiguousarray(dS[r * cfg.Cs:(r + 1) * cfg.Cs]),
        })
    return maps


def kernel(L0_attn_x, L0_attn_W, L0_attn_b, L0_ffn_gate_W_packed, L0_ffn_gate_scales,
           L0_ffn_up_W_packed, L0_ffn_up_scales, L0_ffn_down_W_packed,
           L0_ffn_down_scales):
    x = np.asarray(L0_attn_x)
    B, T, C = x.shape
    N = L0_ffn_gate_W_packed.shape[0]
    cfg = Cfg(T=T, C=C, N=N)
    nc = build_program(cfg)
    maps = _in_maps(cfg, x, np.asarray(L0_attn_W), np.asarray(L0_attn_b),
                    np.asarray(L0_ffn_gate_W_packed), np.asarray(L0_ffn_gate_scales),
                    np.asarray(L0_ffn_up_W_packed), np.asarray(L0_ffn_up_scales),
                    np.asarray(L0_ffn_down_W_packed), np.asarray(L0_ffn_down_scales))
    res = run_bass_kernel_spmd(nc, maps, core_ids=list(range(NCORES)))
    return np.stack([res.results[r]["out"] for r in range(NCORES)], axis=0)


if __name__ == "__main__":
    pass
